# version 03 (final)
# baseline (speedup 1.0000x reference)
import numpy as np
import ml_dtypes
from contextlib import ExitStack

import concourse.bass as bass
import concourse.mybir as mybir
from concourse.bass_utils import run_bass_kernel_spmd

F32 = mybir.dt.float32
BF16 = mybir.dt.bfloat16
AF = mybir.ActivationFunctionType
ALU = mybir.AluOpType
AX = mybir.AxisListType
BFNP = ml_dtypes.bfloat16

NEG = -(2.0 ** 100)
ENGINES = ("sync", "tensor", "vector", "scalar", "gpsimd")
B_, S_, D_ = 4, 4096, 2048
HALF = 2048
NT = 16


class P:
    def __init__(self):
        self.steps = []

    def c(self, eng, *fns):
        fns = [f for f in fns if f is not None]
        if fns:
            self.steps.append((eng, False, fns))

    def d(self, eng, *fns):
        fns = [f for f in fns if f is not None]
        if fns:
            self.steps.append((eng, True, fns))


def emit(nc, prog):
    with (
        nc.semaphore("SC") as SC,
        nc.semaphore("SD") as SD,
        nc.semaphore("SG") as SG,
        nc.Block() as blk,
    ):
        items = {e: [] for e in ENGINES}
        tc = td = tg = 0
        for eng, is_dma, fns in prog.steps:
            items[eng].append((tc, td, tg, is_dma, fns))
            if is_dma and eng == "gpsimd":
                tg += 16 * len(fns)
            elif is_dma:
                td += 16 * len(fns)
            else:
                tc += 1
        tot_c, tot_d, tot_g = tc, td, tg
        assert tot_c < 60000 and tot_d < 60000 and tot_g < 60000, (tot_c, tot_d, tot_g)

        def runner(ename):
            def run(h):
                for (wc, wd, wg, is_dma, fns) in items[ename]:
                    if wc > 0:
                        h.wait_ge(SC, wc)
                    if wd > 0:
                        h.wait_ge(SD, wd)
                    if wg > 0:
                        h.wait_ge(SG, wg)
                    n = len(fns)
                    for i, fn in enumerate(fns):
                        ins = fn(h)
                        if is_dma:
                            ins.then_inc(SG if ename == "gpsimd" else SD, 16)
                        elif i == n - 1:
                            ins.then_inc(SC, 1)
                if ename == "sync":
                    if tot_c > 0:
                        h.wait_ge(SC, tot_c)
                    if tot_d > 0:
                        h.wait_ge(SD, tot_d)
                    if tot_g > 0:
                        h.wait_ge(SG, tot_g)
            return run

        for e in ENGINES:
            if items[e] or e == "sync":
                getattr(blk, e)(runner(e))
    return nc


def _rope_tables(pos, scale):
    invf = 10000.0 ** (-np.arange(64, dtype=np.float64) / 64.0)
    th = np.outer(pos.astype(np.float64), invf)
    c = np.cos(th) * scale
    s = np.sin(th) * scale
    cos = np.concatenate([c, c], axis=1).astype(np.float32)
    sinsw = np.concatenate([s, -s], axis=1).astype(np.float32)
    return cos, sinsw


def _bf(x):
    return np.ascontiguousarray(np.asarray(x).astype(BFNP))


def _f32(x):
    return np.ascontiguousarray(np.asarray(x, dtype=np.float32))


def _tile128(x):
    T, C = x.shape
    n = T // 128
    return np.ascontiguousarray(x.reshape(n, 128, C).transpose(1, 0, 2).reshape(128, n * C))


IDENT = np.eye(128, dtype=np.float32)
_CC = {}


def hostA_consts(half):
    key = ("A", half)
    if key in _CC:
        return _CC[key]
    r0 = half * HALF
    posq = np.arange(r0, r0 + HALF)
    poskv = np.clip(np.arange(r0 - 512, r0 + HALF), 0, None)
    cq, sq = _rope_tables(posq, 0.125)
    ck, sk = _rope_tables(poskv, 1.0)
    cqr = _bf(np.tile(cq, (1, 16)))
    sqr = _bf(np.tile(sq, (1, 16)))
    ckr = _bf(np.tile(ck, (1, 4)))
    skr = _bf(np.tile(sk, (1, 4)))
    qr = np.arange(128)[None, :, None]
    kc = np.arange(640)[None, None, :]
    j = np.arange(16)[:, None, None]
    qg = r0 + j * 128 + qr
    cg = r0 - 512 + j * 128 + kc
    allowed = (cg <= qg) & (cg >= qg - 511) & (cg >= 0)
    maskm = _bf(np.where(allowed, 0.0, NEG))
    _CC[key] = (cqr, sqr, ckr, skr, maskm)
    return _CC[key]


def hostC_consts(half):
    key = ("C", half)
    if key in _CC:
        return _CC[key]
    r0 = half * HALF
    pos = np.arange(r0, r0 + HALF)
    cq, sq = _rope_tables(pos, 128.0 ** -0.5)
    ck, sk = _rope_tables(pos, 1.0)
    cqr = _bf(np.tile(cq, (1, 16)))
    sqr = _bf(np.tile(sq, (1, 16)))
    ckr = _bf(np.tile(ck, (1, 16)))
    skr = _bf(np.tile(sk, (1, 16)))
    qr = np.arange(128)
    a32 = np.zeros((128, 16, 32), np.float32)
    for t in range(16):
        a32[:, t, :] = (np.arange(32)[None, :] == (2 * t + qr[:, None] // 64)) / 64.0
    a32 = _bf(a32.reshape(128, 16 * 32))
    _CC[key] = (cqr, sqr, ckr, skr, a32)
    return _CC[key]


def hostD_consts():
    if "D" in _CC:
        return _CC["D"]
    qr = np.arange(128)
    cmpm = np.zeros((128, 32, 64), np.float32)
    curb = np.zeros((128, 32, 64), np.float32)
    for qt in range(32):
        qabs = qt * 128 + qr
        cmpm[:, qt, :] = np.where(
            qabs[:, None] >= (64 * (np.arange(64)[None, :] + 1) - 1), 0.0, NEG)
        cur = qabs // 64
        curb[:, qt, :] = (cur[:, None] == np.arange(64)[None, :]) * 2.0 \
            - np.arange(64)[None, :] * 2e-7
    cmpm = _bf(cmpm.reshape(128, 32 * 64))
    curb = _f32(curb.reshape(128, 32 * 64))
    cd4 = np.zeros((128, 4, 512), np.float32)
    for m in range(4):
        jj = np.arange(512)[None, :]
        cd4[:, m, :] = np.where(jj <= m * 128 + qr[:, None], 0.0, NEG)
    cd4 = _bf(cd4.reshape(128, 4 * 512))
    swm = np.full((128, 2, 256), NEG, np.float32)
    kcv = np.arange(256)[None, :]
    swm[:, 0, :] = np.where(
        (kcv >= 128 + np.maximum(0, qr - 64)[:, None]) & (kcv <= (128 + qr)[:, None]), 0.0, NEG)
    swm[:, 1, :] = np.where(
        (kcv >= (qr + 64)[:, None]) & (kcv <= (qr + 128)[:, None]), 0.0, NEG)
    swm = _bf(swm.reshape(128, 2 * 256))
    e8 = np.zeros((64, 8, 512), np.float32)
    bb = np.arange(64)[:, None]
    for kcq in range(8):
        e8[:, kcq, :] = np.where(bb == kcq * 8 + (np.arange(512)[None, :] // 64), NEG, 0.0)
    e8 = _bf(e8.reshape(64, 8 * 512))
    a2 = np.zeros((128, 32, 64), np.float32)
    for kt in range(32):
        a2[:, kt, :] = (np.arange(64)[None, :] == (2 * kt + qr[:, None] // 64)) / 64.0
    a2 = _bf(a2.reshape(128, 32 * 64))
    _CC["D"] = (cmpm, curb, cd4, swm, e8, a2)
    return _CC["D"]


def sbuf(ctx, nc, name, shape, dt):
    return ctx.enter_context(nc.sbuf_tensor(name, shape, dt))


def psum(ctx, nc, name, shape, dt):
    return ctx.enter_context(nc.psum_tensor(name, shape, dt))


def ln_block(p, lnb, out_f32, C=2048.0):
    hf, hsum, negm, cen, ss, std, rr = lnb
    p.c("vector", lambda h: h.tensor_scalar_mul(negm[:], hsum[:], -1.0 / C))
    p.c("vector", lambda h: h.scalar_tensor_tensor(cen[:], hf[:], negm[:], hf[:], ALU.add, ALU.bypass))
    p.c("scalar", lambda h: h.activation(hf[:], cen[:], AF.Square, accum_out=ss[:]))
    p.c("vector", lambda h: h.tensor_scalar(std[:], ss[:], 1.0 / C, 1e-5, ALU.mult, ALU.add))
    p.c("scalar", lambda h: h.activation(std[:], std[:], AF.Sqrt))
    p.c("vector", lambda h: h.reciprocal(rr[:], std[:]))
    p.c("vector", lambda h: h.scalar_tensor_tensor(out_f32[:], cen[:], rr[:], cen[:], ALU.mult, ALU.bypass))


def load_cast_w(p, whbm, dst, wf, nch, cols, eng_cycle=("vector", "scalar", "gpsimd")):
    for i in range(nch):
        p.d("sync", lambda h, i=i: h.dma_start(out=wf[:, 0:cols], in_=whbm[i * 128:(i + 1) * 128, :]))
        eng = eng_cycle[i % len(eng_cycle)]
        if eng == "scalar":
            p.c(eng, lambda h, i=i: h.copy(dst[:, i * cols:(i + 1) * cols], wf[:, 0:cols]))
        else:
            p.c(eng, lambda h, i=i: h.tensor_copy(dst[:, i * cols:(i + 1) * cols], wf[:, 0:cols]))


def emit_A(nc, p, xkv, wq, wk, wv, lamb, cqr, sqr, ckr, skr, maskm, ident, oT):
    with ExitStack() as ctx:
        i_f = sbuf(ctx, nc, "A_i_f", [128, 128], F32)
        i_b = sbuf(ctx, nc, "A_i_b", [128, 128], BF16)
        lam_s = sbuf(ctx, nc, "A_lam_s", [128, 16], F32)
        nlam = sbuf(ctx, nc, "A_nlam", [128, 16], F32)
        WQb = sbuf(ctx, nc, "A_WQb", [128, 16 * 2048], BF16)
        WKb = sbuf(ctx, nc, "A_WKb", [128, 16 * 512], BF16)
        WVb = sbuf(ctx, nc, "A_WVb", [128, 16 * 512], BF16)
        wf = sbuf(ctx, nc, "A_wf", [128, 2048], F32)
        xf = sbuf(ctx, nc, "A_xf", [128, 2048], F32)
        xb = sbuf(ctx, nc, "A_xb", [128, 2048], BF16)
        XT = sbuf(ctx, nc, "A_XT", [128, 2048], BF16)
        KTr = sbuf(ctx, nc, "A_KTr", [128, 4 * 6 * 128], BF16)
        Vr = sbuf(ctx, nc, "A_Vr", [128, 6 * 512], BF16)
        ctab = sbuf(ctx, nc, "A_ctab", [128, 2048], BF16)
        stab = sbuf(ctx, nc, "A_stab", [128, 2048], BF16)
        tt1 = sbuf(ctx, nc, "A_tt1", [128, 2048], F32)
        tt2 = sbuf(ctx, nc, "A_tt2", [128, 2048], F32)
        kb = sbuf(ctx, nc, "A_kb", [128, 512], BF16)
        qb = sbuf(ctx, nc, "A_qb", [128, 2048], BF16)
        QT = sbuf(ctx, nc, "A_QT", [128, 2048], BF16)
        OTs = sbuf(ctx, nc, "A_OTs", [128, 2048], BF16)
        mk = sbuf(ctx, nc, "A_mk", [128, 640], BF16)
        e1b = sbuf(ctx, nc, "A_e1b", [128, 640], BF16)
        e2b = sbuf(ctx, nc, "A_e2b", [128, 640], BF16)
        a1b = sbuf(ctx, nc, "A_a1b", [128, 640], BF16)
        dfb = sbuf(ctx, nc, "A_dfb", [128, 640], BF16)
        dfr = sbuf(ctx, nc, "A_dfr", [128, 640], BF16)
        dfn = sbuf(ctx, nc, "A_dfn", [128, 640], BF16)
        PTs = sbuf(ctx, nc, "A_PTs", [128, 640], BF16)
        s1 = sbuf(ctx, nc, "A_s1", [128, 1], F32)
        s2 = sbuf(ctx, nc, "A_s2", [128, 1], F32)
        r1 = sbuf(ctx, nc, "A_r1", [128, 1], F32)
        r2 = sbuf(ctx, nc, "A_r2", [128, 1], F32)
        lr2 = sbuf(ctx, nc, "A_lr2", [128, 1], F32)
        dsv = sbuf(ctx, nc, "A_dsv", [128, 1], F32)
        dsp = sbuf(ctx, nc, "A_dsp", [128, 1], F32)
        rdv = sbuf(ctx, nc, "A_rdv", [128, 1], F32)
        PAf = psum(ctx, nc, "A_PAf", [128, 2048], F32)
        PBb = psum(ctx, nc, "A_PBb", [128, 2048], BF16)

        p.d("sync",
            lambda h: h.dma_start(out=i_f[:], in_=ident),
            lambda h: h.dma_start(out=lam_s[:], in_=lamb))
        p.c("vector", lambda h: h.tensor_copy(i_b[:], i_f[:]))
        p.c("vector", lambda h: h.tensor_scalar_mul(nlam[:], lam_s[:], -1.0))
        load_cast_w(p, wq, WQb, wf, 16, 2048)
        load_cast_w(p, wk, WKb, wf, 16, 512)
        load_cast_w(p, wv, WVb, wf, 16, 512)

        for kvt in range(20):
            slot = kvt % 6
            p.d("sync", lambda h, kvt=kvt: h.dma_start(out=xf[:], in_=xkv[kvt * 128:(kvt + 1) * 128, :]))
            p.c("vector", lambda h: h.tensor_copy(xb[:], xf[:]))
            p.c("tensor", *[(lambda h, i=i: h.matmul(
                    PBb[:, i * 128:(i + 1) * 128], xb[:, i * 128:(i + 1) * 128], i_b[:],
                    is_transpose=True, start=(i % 8 == 0), stop=(i % 8 == 7))) for i in range(16)])
            p.c("vector", lambda h: h.tensor_copy(XT[:], PBb[:]))
            p.c("tensor",
                *[(lambda h, i=i: h.matmul(PAf[:, 0:512], XT[:, i * 128:(i + 1) * 128],
                                           WKb[:, i * 512:(i + 1) * 512],
                                           start=(i == 0), stop=(i == 15))) for i in range(16)],
                *[(lambda h, i=i: h.matmul(PAf[:, 512:1024], XT[:, i * 128:(i + 1) * 128],
                                           WVb[:, i * 512:(i + 1) * 512],
                                           start=(i == 0), stop=(i == 15))) for i in range(16)])
            p.d("sync",
                lambda h, kvt=kvt: h.dma_start(out=ctab[:, 0:512], in_=ckr[kvt * 128:(kvt + 1) * 128, :]),
                lambda h, kvt=kvt: h.dma_start(out=stab[:, 0:512], in_=skr[kvt * 128:(kvt + 1) * 128, :]))
            p.c("vector", lambda h: h.tensor_tensor(tt1[:, 0:512], PAf[:, 0:512], ctab[:, 0:512], ALU.mult))
            p.c("vector", lambda h: h.tensor_tensor(tt2[:, 0:512], PAf[:, 0:512], stab[:, 0:512], ALU.mult))
            p.c("vector", lambda h: h.tensor_tensor(
                kb[:].rearrange("p (hh two s) -> p hh two s", hh=4, two=2),
                tt1[:, 0:512].rearrange("p (hh two s) -> p hh two s", hh=4, two=2),
                tt2[:, 0:512].rearrange("p (hh two s) -> p hh two s", hh=4, two=2)[:, :, ::-1, :],
                ALU.add))
            p.c("scalar", lambda h, slot=slot: h.copy(Vr[:, slot * 512:(slot + 1) * 512], PAf[:, 512:1024]))
            p.c("tensor", *[(lambda h, hk=hk: h.matmul(
                PBb[:, hk * 128:(hk + 1) * 128], kb[:, hk * 128:(hk + 1) * 128], i_b[:],
                is_transpose=True, start=(hk == 0), stop=(hk == 3))) for hk in range(4)])
            p.c("vector", lambda h, slot=slot: h.tensor_copy(
                KTr[:].rearrange("p (hh ss q) -> p hh ss q", hh=4, ss=6)[:, :, slot, :],
                PBb[:, 0:512].rearrange("p (hh q) -> p hh q", hh=4)))

            if kvt < 4:
                continue
            j = kvt - 4
            p.c("tensor", *[(lambda h, n=n, i=i: h.matmul(
                PAf[:, n * 512:(n + 1) * 512], XT[:, i * 128:(i + 1) * 128],
                WQb[:, i * 2048 + n * 512: i * 2048 + n * 512 + 512],
                start=(i == 0), stop=(i == 15))) for n in range(4) for i in range(16)])
            p.d("sync",
                lambda h, j=j: h.dma_start(out=ctab[:], in_=cqr[j * 128:(j + 1) * 128, :]),
                lambda h, j=j: h.dma_start(out=stab[:], in_=sqr[j * 128:(j + 1) * 128, :]))
            p.c("vector", lambda h: h.tensor_tensor(tt1[:], PAf[:], ctab[:], ALU.mult))
            p.c("vector", lambda h: h.tensor_tensor(tt2[:], PAf[:], stab[:], ALU.mult))
            p.c("vector", lambda h: h.tensor_tensor(
                qb[:].rearrange("p (hh two s) -> p hh two s", hh=16, two=2),
                tt1[:].rearrange("p (hh two s) -> p hh two s", hh=16, two=2),
                tt2[:].rearrange("p (hh two s) -> p hh two s", hh=16, two=2)[:, :, ::-1, :],
                ALU.add))
            p.c("tensor", *[(lambda h, i=i: h.matmul(
                    PBb[:, i * 128:(i + 1) * 128], qb[:, i * 128:(i + 1) * 128], i_b[:],
                    is_transpose=True, start=(i % 8 == 0), stop=(i % 8 == 7))) for i in range(16)])
            p.c("vector", lambda h: h.tensor_copy(QT[:], PBb[:]))
            p.d("sync", lambda h, j=j: h.dma_start(out=mk[:], in_=maskm[j]))
            for hh in range(16):
                hk = hh // 4
                def mk_scores(base, lo, hi, hh=hh, hk=hk, j=j):
                    fns = []
                    for c in range(5):
                        sl = (j + c) % 6
                        fns.append(lambda h, c=c, sl=sl, base=base, lo=lo, hi=hi, hh=hh, hk=hk:
                                   h.matmul(PAf[:, base + c * 128: base + c * 128 + 128],
                                            QT[lo:hi, hh * 128:(hh + 1) * 128],
                                            KTr[lo:hi, (hk * 6 + sl) * 128:(hk * 6 + sl) * 128 + 128],
                                            start=(c == 0 or c == 4), stop=False))
                    fns.append(lambda h, base=base: h.matmul(
                        PAf[:, base: base + 512], i_b[:], mk[:, 0:512], start=False, stop=True))
                    fns.append(lambda h, base=base: h.matmul(
                        PAf[:, base + 512: base + 640], i_b[:], mk[:, 512:640], start=False, stop=True))
                    return fns
                p.c("tensor", *mk_scores(0, 0, 64), *mk_scores(1024, 64, 128))
                p.c("scalar", lambda h: h.activation(e1b[:], PAf[:, 0:640], AF.Exp, accum_out=s1[:]))
                p.c("scalar", lambda h: h.activation(e2b[:], PAf[:, 1024:1664], AF.Exp, accum_out=s2[:]))
                p.c("vector",
                    lambda h: h.reciprocal(r1[:], s1[:]),
                    lambda h: h.reciprocal(r2[:], s2[:]))
                p.c("vector", lambda h, hh=hh: h.tensor_tensor(lr2[:], r2[:], nlam[:, hh:hh + 1], ALU.mult))
                p.c("vector", lambda h: h.scalar_tensor_tensor(a1b[:], e1b[:], r1[:], e1b[:], ALU.mult, ALU.bypass))
                p.c("vector", lambda h: h.scalar_tensor_tensor(dfb[:], e2b[:], lr2[:], a1b[:], ALU.mult, ALU.add))
                p.c("scalar", lambda h: h.activation(dfr[:], dfb[:], AF.Relu, accum_out=dsv[:]))
                p.c("vector", lambda h: h.tensor_scalar(dsp[:], dsv[:], 1e-6, None, ALU.add))
                p.c("vector", lambda h: h.reciprocal(rdv[:], dsp[:]))
                p.c("vector", lambda h: h.scalar_tensor_tensor(dfn[:], dfr[:], rdv[:], dfr[:], ALU.mult, ALU.bypass))
                p.c("tensor", *[(lambda h, c=c: h.matmul(
                    PBb[:, c * 128:(c + 1) * 128], dfn[:, c * 128:(c + 1) * 128], i_b[:],
                    is_transpose=True, start=(c == 0), stop=(c == 4))) for c in range(5)])
                p.c("scalar", lambda h: h.copy(PTs[:], PBb[:, 0:640]))
                p.c("tensor", *[(lambda h, c=c, hk=hk, j=j: h.matmul(
                    PAf[:, 1664:1792],
                    Vr[:, ((j + c) % 6) * 512 + hk * 128: ((j + c) % 6) * 512 + hk * 128 + 128],
                    PTs[:, c * 128:(c + 1) * 128],
                    start=(c == 0), stop=(c == 4))) for c in range(5)])
                p.c("scalar", lambda h, hh=hh: h.copy(OTs[:, hh * 128:(hh + 1) * 128], PAf[:, 1664:1792]))
            p.d("gpsimd", lambda h, j=j: h.dma_start(
                out=oT[j].rearrange("hh d q -> d hh q"),
                in_=OTs[:].rearrange("p (hh q) -> p hh q", hh=16)))


def emit_B(nc, p, oTd, xres, wo, clsT, ebias, gwT, uwT, dwT, ident, y2o, y1h):
    with ExitStack() as ctx:
        i_f = sbuf(ctx, nc, "B_i_f", [128, 128], F32)
        i_b = sbuf(ctx, nc, "B_i_b", [128, 128], BF16)
        y1T = sbuf(ctx, nc, "B_y1T", [128, 16 * 2048], BF16)
        gates = sbuf(ctx, nc, "B_gates", [128, 16 * 12], F32)
        zp = sbuf(ctx, nc, "B_zp", [128, 16], F32)
        zp1 = sbuf(ctx, nc, "B_zp1", [128, 1], F32)
        ebs = sbuf(ctx, nc, "B_ebs", [128, 12], F32)
        PB1 = psum(ctx, nc, "B_PB1", [128, 2048], F32)
        PG = psum(ctx, nc, "B_PG", [128, 512], F32)
        PU = psum(ctx, nc, "B_PU", [128, 512], F32)
        PBb = psum(ctx, nc, "B_PBb", [128, 2048], BF16)

        p.d("sync", lambda h: h.dma_start(out=i_f[:], in_=ident))
        p.c("vector", lambda h: h.tensor_copy(i_b[:], i_f[:]))
        with ExitStack() as c1:
            WOb = sbuf(c1, nc, "B_WOb", [128, 16 * 2048], BF16)
            wf = sbuf(c1, nc, "B_wf", [128, 2048], F32)
            OTb = sbuf(c1, nc, "B_OTb", [128, 2048], BF16)
            ynb = sbuf(c1, nc, "B_ynb", [128, 2048], BF16)
            hf = sbuf(c1, nc, "B_hf", [128, 2048], F32)
            cen = sbuf(c1, nc, "B_cen", [128, 2048], F32)
            ynf = sbuf(c1, nc, "B_ynf", [128, 2048], F32)
            hsum = sbuf(c1, nc, "B_hsum", [128, 1], F32)
            negm = sbuf(c1, nc, "B_negm", [128, 1], F32)
            ss = sbuf(c1, nc, "B_ss", [128, 1], F32)
            std = sbuf(c1, nc, "B_std", [128, 1], F32)
            rr = sbuf(c1, nc, "B_rr", [128, 1], F32)
            lnb = (hf, hsum, negm, cen, ss, std, rr)
            load_cast_w(p, wo, WOb, wf, 16, 2048)
            for t in range(16):
                p.d("sync", lambda h, t=t: h.dma_start(
                    out=OTb[:].rearrange("p (hh q) -> p hh q", hh=16),
                    in_=oTd[t].rearrange("hh d q -> d hh q")))
                p.c("tensor", *[(lambda h, n=n, hh=hh: h.matmul(
                    PB1[:, n * 512:(n + 1) * 512], OTb[:, hh * 128:(hh + 1) * 128],
                    WOb[:, hh * 2048 + n * 512: hh * 2048 + n * 512 + 512],
                    start=(hh == 0), stop=(hh == 15))) for n in range(4) for hh in range(16)])
                p.d("sync", lambda h, t=t: h.dma_start(out=wf[:], in_=xres[t * 128:(t + 1) * 128, :]))
                p.c("vector", lambda h: h.scalar_tensor_tensor(
                    hf[:], PB1[:], 1.0, wf[:], ALU.mult, ALU.add, accum_out=hsum[:]))
                ln_block(p, lnb, ynf)
                p.c("scalar", lambda h: h.activation(ynb[:], ynf[:], AF.Copy))
                p.c("tensor", *[(lambda h, i=i: h.matmul(
                    PBb[:, i * 128:(i + 1) * 128], ynb[:, i * 128:(i + 1) * 128], i_b[:],
                    is_transpose=True, start=(i % 8 == 0), stop=(i % 8 == 7))) for i in range(16)])
                p.c("vector", lambda h, t=t: h.tensor_copy(y1T[:, t * 2048:(t + 1) * 2048], PBb[:]))
                p.d("gpsimd", lambda h, t=t: h.dma_start(out=y1h[t * 128:(t + 1) * 128, :], in_=ynf[:]))

        with ExitStack() as c23:
            ACC = sbuf(c23, nc, "B_ACC", [128, 16 * 2048], BF16)
            with ExitStack() as c2:
                clsb = sbuf(c2, nc, "B_clsb", [128, 192], BF16)
                wstage = sbuf(c2, nc, "B_wstage", [128, 2048], F32)
                gwhb = sbuf(c2, nc, "B_gwhb", [128, 16 * 512], BF16)
                uwhb = sbuf(c2, nc, "B_uwhb", [128, 16 * 512], BF16)
                dwhb = sbuf(c2, nc, "B_dwhb", [128, 4 * 2048], BF16)
                sgf = sbuf(c2, nc, "B_sgf", [128, 512], F32)
                actb = sbuf(c2, nc, "B_actb", [128, 512], BF16)
                atb = sbuf(c2, nc, "B_atb", [128, 512], BF16)
                elf = sbuf(c2, nc, "B_elf", [128, 12], F32)
                bsd = sbuf(c2, nc, "B_bsd", [128, 12], F32)
                self_ = sbuf(c2, nc, "B_self_", [128, 12], F32)
                g1f = sbuf(c2, nc, "B_g1f", [128, 12], F32)
                esm = sbuf(c2, nc, "B_esm", [128, 1], F32)
                rcl = sbuf(c2, nc, "B_rcl", [128, 1], F32)
                m8g = sbuf(c2, nc, "B_m8g", [128, 8], F32)
                p.d("sync",
                    lambda h: h.dma_start(out=wstage[:, 0:192], in_=clsT),
                    lambda h: h.dma_start(out=ebs[:], in_=ebias))
                p.c("vector", lambda h: h.tensor_copy(clsb[:], wstage[:, 0:192]))
                for t in range(16):
                    p.c("tensor", *[(lambda h, t=t, i=i: h.matmul(
                        PG[:, 0:12], y1T[:, t * 2048 + i * 128: t * 2048 + i * 128 + 128],
                        clsb[:, i * 12:(i + 1) * 12],
                        start=(i == 0), stop=(i == 15))) for i in range(16)])
                    p.c("scalar", lambda h: h.activation(elf[:], PG[:, 0:12], AF.Exp, accum_out=esm[:]))
                    p.c("vector", lambda h: h.reciprocal(rcl[:], esm[:]))
                    p.c("vector", lambda h: h.scalar_tensor_tensor(bsd[:], elf[:], rcl[:], ebs[:], ALU.mult, ALU.add))
                    p.c("vector", lambda h: h.max(out=m8g[:], in_=bsd[:]))
                    p.c("vector", lambda h: h.scalar_tensor_tensor(self_[:], bsd[:], m8g[:, 1:2], bsd[:], ALU.is_ge, ALU.bypass))
                    p.c("vector", lambda h: h.scalar_tensor_tensor(g1f[:], elf[:], rcl[:], self_[:], ALU.mult, ALU.mult))
                    p.c("vector", lambda h, t=t: h.tensor_scalar_mul(gates[:, t * 12:(t + 1) * 12], g1f[:], 1.5))
                    p.c("vector", lambda h, t=t: h.tensor_reduce(
                        zp[:, t:t + 1], gates[:, t * 12 + 8:t * 12 + 12], axis=AX.X, op=ALU.add))
                p.c("vector", lambda h: h.memset(ACC[:], 0.0))
                for e in range(8):
                    for hl in range(2):
                        for g4 in range(4):
                            p.d("sync", lambda h, e=e, hl=hl, g4=g4: h.dma_start(
                                out=wstage[:].rearrange("p (c w) -> p c w", c=4),
                                in_=gwT[e, g4 * 4:(g4 + 1) * 4, :, hl * 512:(hl + 1) * 512].rearrange("c p w -> p c w")))
                            if g4 % 2 == 0:
                                p.c("vector", lambda h, g4=g4: h.tensor_copy(gwhb[:, g4 * 2048:(g4 + 1) * 2048], wstage[:]))
                            else:
                                p.c("scalar", lambda h, g4=g4: h.copy(gwhb[:, g4 * 2048:(g4 + 1) * 2048], wstage[:]))
                            p.d("sync", lambda h, e=e, hl=hl, g4=g4: h.dma_start(
                                out=wstage[:].rearrange("p (c w) -> p c w", c=4),
                                in_=uwT[e, g4 * 4:(g4 + 1) * 4, :, hl * 512:(hl + 1) * 512].rearrange("c p w -> p c w")))
                            if g4 % 2 == 0:
                                p.c("scalar", lambda h, g4=g4: h.copy(uwhb[:, g4 * 2048:(g4 + 1) * 2048], wstage[:]))
                            else:
                                p.c("vector", lambda h, g4=g4: h.tensor_copy(uwhb[:, g4 * 2048:(g4 + 1) * 2048], wstage[:]))
                        for ic in range(4):
                            p.d("sync", lambda h, e=e, hl=hl, ic=ic: h.dma_start(
                                out=wstage[:], in_=dwT[e, hl * 4 + ic]))
                            p.c("gpsimd", lambda h, ic=ic: h.tensor_copy(dwhb[:, ic * 2048:(ic + 1) * 2048], wstage[:]))
                        for t in range(16):
                            p.c("tensor",
                                *[(lambda h, t=t, i=i: h.matmul(
                                    PG[:], y1T[:, t * 2048 + i * 128: t * 2048 + i * 128 + 128],
                                    gwhb[:, i * 512:(i + 1) * 512],
                                    start=(i == 0), stop=(i == 15))) for i in range(16)],
                                *[(lambda h, t=t, i=i: h.matmul(
                                    PU[:], y1T[:, t * 2048 + i * 128: t * 2048 + i * 128 + 128],
                                    uwhb[:, i * 512:(i + 1) * 512],
                                    start=(i == 0), stop=(i == 15))) for i in range(16)])
                            p.c("scalar", lambda h: h.activation(sgf[:], PG[:], AF.Silu))
                            p.c("vector", lambda h, t=t, e=e: h.scalar_tensor_tensor(
                                actb[:], sgf[:], gates[:, t * 12 + e: t * 12 + e + 1], PU[:], ALU.mult, ALU.mult))
                            p.c("tensor", *[(lambda h, c=c: h.matmul(
                                PBb[:, c * 128:(c + 1) * 128], actb[:, c * 128:(c + 1) * 128], i_b[:],
                                is_transpose=True, start=(c == 0), stop=(c == 3))) for c in range(4)])
                            p.c("scalar", lambda h: h.copy(atb[:], PBb[:, 0:512]))
                            p.c("tensor", *[(lambda h, n=n, ic=ic: h.matmul(
                                PB1[:, n * 512:(n + 1) * 512], atb[:, ic * 128:(ic + 1) * 128],
                                dwhb[:, ic * 2048 + n * 512: ic * 2048 + n * 512 + 512],
                                start=(ic == 0), stop=(ic == 3))) for n in range(4) for ic in range(4)])
                            p.c("vector", lambda h, t=t: h.tensor_tensor(
                                ACC[:, t * 2048:(t + 1) * 2048], ACC[:, t * 2048:(t + 1) * 2048], PB1[:], ALU.add))
            with ExitStack() as c3:
                y1f = sbuf(c3, nc, "B_y1f", [128, 2048], F32)
                hf3 = sbuf(c3, nc, "B_hf3", [128, 2048], F32)
                cen3 = sbuf(c3, nc, "B_cen3", [128, 2048], F32)
                ynf3 = sbuf(c3, nc, "B_ynf3", [128, 2048], F32)
                hsum3 = sbuf(c3, nc, "B_hsum3", [128, 1], F32)
                negm3 = sbuf(c3, nc, "B_negm3", [128, 1], F32)
                ss3 = sbuf(c3, nc, "B_ss3", [128, 1], F32)
                std3 = sbuf(c3, nc, "B_std3", [128, 1], F32)
                rr3 = sbuf(c3, nc, "B_rr3", [128, 1], F32)
                lnb3 = (hf3, hsum3, negm3, cen3, ss3, std3, rr3)
                for t in range(16):
                    p.d("sync", lambda h, t=t: h.dma_start(out=y1f[:], in_=y1h[t * 128:(t + 1) * 128, :]))
                    p.c("vector", lambda h, t=t: h.tensor_scalar(zp1[:], zp[:, t:t + 1], 1.0, None, ALU.add))
                    p.c("vector", lambda h, t=t: h.scalar_tensor_tensor(
                        hf3[:], y1f[:], zp1[:], ACC[:, t * 2048:(t + 1) * 2048], ALU.mult, ALU.add,
                        accum_out=hsum3[:]))
                    ln_block(p, lnb3, ynf3)
                    p.d("gpsimd", lambda h, t=t: h.dma_start(out=y2o[t * 128:(t + 1) * 128, :], in_=ynf3[:]))


def emit_C(nc, p, y2i, wq1, wk1, wv1, wg1, cqr, sqr, ckr, skr, a32d, ident, qTo, kTo, vo, go, kco):
    with ExitStack() as ctx:
        i_f = sbuf(ctx, nc, "C_i_f", [128, 128], F32)
        i_b = sbuf(ctx, nc, "C_i_b", [128, 128], BF16)
        Wb = sbuf(ctx, nc, "C_Wb", [128, 16 * 2048], BF16)
        wgb = sbuf(ctx, nc, "C_wgb", [128, 16 * 48], BF16)
        a32s = sbuf(ctx, nc, "C_a32s", [128, 16 * 32], BF16)
        wf = sbuf(ctx, nc, "C_wf", [128, 2048], F32)
        xf = sbuf(ctx, nc, "C_xf", [128, 2048], F32)
        xb = sbuf(ctx, nc, "C_xb", [128, 2048], BF16)
        XT = sbuf(ctx, nc, "C_XT", [128, 2048], BF16)
        ctab = sbuf(ctx, nc, "C_ctab", [128, 2048], BF16)
        stab = sbuf(ctx, nc, "C_stab", [128, 2048], BF16)
        tt1 = sbuf(ctx, nc, "C_tt1", [128, 2048], F32)
        tt2 = sbuf(ctx, nc, "C_tt2", [128, 2048], F32)
        rb = sbuf(ctx, nc, "C_rb", [128, 2048], BF16)
        OTt = sbuf(ctx, nc, "C_OTt", [128, 2048], BF16)
        ggf = sbuf(ctx, nc, "C_ggf", [128, 48], F32)
        kcf = sbuf(ctx, nc, "C_kcf", [128, 512], F32)
        PAf = psum(ctx, nc, "C_PAf", [128, 2048], F32)
        PBb = psum(ctx, nc, "C_PBb", [128, 2048], BF16)
        pkc = psum(ctx, nc, "C_pkc", [128, 512], F32)
        pgp = psum(ctx, nc, "C_pgp", [128, 64], F32)

        p.d("sync",
            lambda h: h.dma_start(out=i_f[:], in_=ident),
            lambda h: h.dma_start(out=a32s[:], in_=a32d))
        p.c("vector", lambda h: h.tensor_copy(i_b[:], i_f[:]))
        for pas, wsrc in ((0, wq1), (1, wk1), (2, wv1)):
            load_cast_w(p, wsrc, Wb, wf, 16, 2048)
            if pas == 2:
                p.d("sync", lambda h: h.dma_start(out=wf[:, 0:768], in_=wg1))
                p.c("vector", lambda h: h.tensor_copy(wgb[:], wf[:, 0:768]))
            for t in range(16):
                p.d("sync", lambda h, t=t: h.dma_start(out=xf[:], in_=y2i[t * 128:(t + 1) * 128, :]))
                p.c("vector", lambda h: h.tensor_copy(xb[:], xf[:]))
                p.c("tensor", *[(lambda h, i=i: h.matmul(
                        PBb[:, i * 128:(i + 1) * 128], xb[:, i * 128:(i + 1) * 128], i_b[:],
                        is_transpose=True, start=(i % 8 == 0), stop=(i % 8 == 7))) for i in range(16)])
                p.c("vector", lambda h: h.tensor_copy(XT[:], PBb[:]))
                p.c("tensor", *[(lambda h, n=n, i=i: h.matmul(
                    PAf[:, n * 512:(n + 1) * 512], XT[:, i * 128:(i + 1) * 128],
                    Wb[:, i * 2048 + n * 512: i * 2048 + n * 512 + 512],
                    start=(i == 0), stop=(i == 15))) for n in range(4) for i in range(16)])
                if pas in (0, 1):
                    csrc, ssrc = (cqr, sqr) if pas == 0 else (ckr, skr)
                    p.d("sync",
                        lambda h, t=t, csrc=csrc: h.dma_start(out=ctab[:], in_=csrc[t * 128:(t + 1) * 128, :]),
                        lambda h, t=t, ssrc=ssrc: h.dma_start(out=stab[:], in_=ssrc[t * 128:(t + 1) * 128, :]))
                    p.c("vector", lambda h: h.tensor_tensor(tt1[:], PAf[:], ctab[:], ALU.mult))
                    p.c("vector", lambda h: h.tensor_tensor(tt2[:], PAf[:], stab[:], ALU.mult))
                    p.c("vector", lambda h: h.tensor_tensor(
                        rb[:].rearrange("p (hh two s) -> p hh two s", hh=16, two=2),
                        tt1[:].rearrange("p (hh two s) -> p hh two s", hh=16, two=2),
                        tt2[:].rearrange("p (hh two s) -> p hh two s", hh=16, two=2)[:, :, ::-1, :],
                        ALU.add))
                    if pas == 1:
                        p.c("tensor", *[(lambda h, t=t, hh=hh: h.matmul(
                            pkc[:, hh * 32:(hh + 1) * 32], rb[:, hh * 128:(hh + 1) * 128],
                            a32s[:, t * 32:(t + 1) * 32],
                            start=(t == 0 and hh == 0),
                            stop=(t == 15 and hh == 15))) for hh in range(16)])
                    p.c("tensor", *[(lambda h, i=i: h.matmul(
                            PBb[:, i * 128:(i + 1) * 128], rb[:, i * 128:(i + 1) * 128], i_b[:],
                            is_transpose=True, start=(i % 8 == 0), stop=(i % 8 == 7))) for i in range(16)])
                    p.c("vector", lambda h: h.tensor_copy(OTt[:], PBb[:]))
                    dst = qTo if pas == 0 else kTo
                    p.d("gpsimd", lambda h, t=t, dst=dst: h.dma_start(
                        out=dst[:, :, t * 128:(t + 1) * 128].rearrange("hh d q -> d hh q"),
                        in_=OTt[:].rearrange("p (hh q) -> p hh q", hh=16)))
                else:
                    p.c("vector", lambda h: h.tensor_copy(rb[:], PAf[:]))
                    p.d("gpsimd", lambda h, t=t: h.dma_start(out=vo[t * 128:(t + 1) * 128, :], in_=rb[:]))
                    p.c("tensor", *[(lambda h, i=i: h.matmul(
                        pgp[:, 0:48], XT[:, i * 128:(i + 1) * 128], wgb[:, i * 48:(i + 1) * 48],
                        start=(i == 0), stop=(i == 15))) for i in range(16)])
                    p.c("scalar", lambda h: h.activation(ggf[:, 0:48], pgp[:, 0:48], AF.Sigmoid))
                    p.d("gpsimd", lambda h, t=t: h.dma_start(out=go[t * 128:(t + 1) * 128, :], in_=ggf[:, 0:48]))
        p.c("vector", lambda h: h.tensor_copy(kcf[:], pkc[:]))
        p.d("gpsimd", lambda h: h.dma_start(out=kco, in_=kcf[:]))


def build_ABC():
    nc = bass.Bass("TRN2", target_bir_lowering=False)
    xkv = nc.dram_tensor("xkv", [2560, D_], F32, kind="ExternalInput").ap()
    wq = nc.dram_tensor("wq", [D_, 2048], F32, kind="ExternalInput").ap()
    wk = nc.dram_tensor("wk", [D_, 512], F32, kind="ExternalInput").ap()
    wv = nc.dram_tensor("wv", [D_, 512], F32, kind="ExternalInput").ap()
    lamb = nc.dram_tensor("lamb", [128, 16], F32, kind="ExternalInput").ap()
    cqr0 = nc.dram_tensor("cqr0", [HALF, 2048], BF16, kind="ExternalInput").ap()
    sqr0 = nc.dram_tensor("sqr0", [HALF, 2048], BF16, kind="ExternalInput").ap()
    ckr0 = nc.dram_tensor("ckr0", [2560, 512], BF16, kind="ExternalInput").ap()
    skr0 = nc.dram_tensor("skr0", [2560, 512], BF16, kind="ExternalInput").ap()
    maskm = nc.dram_tensor("maskm", [16, 128, 640], BF16, kind="ExternalInput").ap()
    ident = nc.dram_tensor("ident", [128, 128], F32, kind="ExternalInput").ap()
    oT = nc.dram_tensor("oT", [16, 16, 128, 128], BF16, kind="Internal").ap()
    wo = nc.dram_tensor("wo", [D_, D_], F32, kind="ExternalInput").ap()
    clsT = nc.dram_tensor("clsT", [128, 192], F32, kind="ExternalInput").ap()
    ebias = nc.dram_tensor("ebias", [128, 12], F32, kind="ExternalInput").ap()
    gwT = nc.dram_tensor("gwT", [8, 16, 128, 1024], F32, kind="ExternalInput").ap()
    uwT = nc.dram_tensor("uwT", [8, 16, 128, 1024], F32, kind="ExternalInput").ap()
    dwT = nc.dram_tensor("dwT", [8, 8, 128, 2048], F32, kind="ExternalInput").ap()
    y2o = nc.dram_tensor("y2o", [HALF, D_], F32, kind="ExternalOutput").ap()
    y1h = nc.dram_tensor("y1h", [HALF, D_], F32, kind="Internal").ap()
    wq1 = nc.dram_tensor("wq1", [D_, D_], F32, kind="ExternalInput").ap()
    wk1 = nc.dram_tensor("wk1", [D_, D_], F32, kind="ExternalInput").ap()
    wv1 = nc.dram_tensor("wv1", [D_, D_], F32, kind="ExternalInput").ap()
    wg1 = nc.dram_tensor("wg1", [128, 16 * 48], F32, kind="ExternalInput").ap()
    cqr1 = nc.dram_tensor("cqr1", [HALF, 2048], BF16, kind="ExternalInput").ap()
    sqr1 = nc.dram_tensor("sqr1", [HALF, 2048], BF16, kind="ExternalInput").ap()
    ckr1 = nc.dram_tensor("ckr1", [HALF, 2048], BF16, kind="ExternalInput").ap()
    skr1 = nc.dram_tensor("skr1", [HALF, 2048], BF16, kind="ExternalInput").ap()
    a32d = nc.dram_tensor("a32d", [128, 16 * 32], BF16, kind="ExternalInput").ap()
    qTo = nc.dram_tensor("qTo", [16, 128, HALF], BF16, kind="ExternalOutput").ap()
    kTo = nc.dram_tensor("kTo", [16, 128, HALF], BF16, kind="ExternalOutput").ap()
    vo = nc.dram_tensor("vo", [HALF, D_], BF16, kind="ExternalOutput").ap()
    go = nc.dram_tensor("go", [HALF, 48], F32, kind="ExternalOutput").ap()
    kco = nc.dram_tensor("kco", [128, 512], F32, kind="ExternalOutput").ap()

    p = P()
    emit_A(nc, p, xkv, wq, wk, wv, lamb, cqr0, sqr0, ckr0, skr0, maskm, ident, oT)
    emit_B(nc, p, oT, xkv[512:2560, :], wo, clsT, ebias, gwT, uwT, dwT, ident, y2o, y1h)
    emit_C(nc, p, y2o, wq1, wk1, wv1, wg1, cqr1, sqr1, ckr1, skr1, a32d, ident, qTo, kTo, vo, go, kco)
    emit(nc, p)
    return nc


def prepABC(inputs, core):
    m = prepA(inputs, core)
    m["cqr0"] = m.pop("cqr"); m["sqr0"] = m.pop("sqr")
    m["ckr0"] = m.pop("ckr"); m["skr0"] = m.pop("skr")
    mb = prepB(inputs, core, None)
    for k in ("wo", "clsT", "ebias", "gwT", "uwT", "dwT"):
        m[k] = mb[k]
    mc = prepC(inputs, core, None)
    for k in ("wq1", "wk1", "wv1", "wg1", "a32d"):
        m[k] = mc[k]
    m["cqr1"] = mc["cqr"]; m["sqr1"] = mc["sqr"]
    m["ckr1"] = mc["ckr"]; m["skr1"] = mc["skr"]
    return m


def prepA(inputs, core):
    b, half = core // 2, core % 2
    r0 = half * HALF
    x = inputs["L0_attn_x"]
    xkv = np.zeros((2560, D_), np.float32)
    lo = r0 - 512
    src_lo = max(0, lo)
    xkv[src_lo - lo:] = x[b, src_lo:r0 + HALF]
    cqr, sqr, ckr, skr, maskm = hostA_consts(half)
    return {
        "xkv": _f32(xkv), "wq": _f32(inputs["L0_attn_Wq"]), "wk": _f32(inputs["L0_attn_Wk"]),
        "wv": _f32(inputs["L0_attn_Wv"]),
        "lamb": _f32(np.tile(inputs["L0_attn_lam"].reshape(1, 16), (128, 1))),
        "cqr": cqr, "sqr": sqr, "ckr": ckr, "skr": skr, "maskm": maskm, "ident": IDENT,
    }


def prepB(inputs, core, oT_core):
    b, half = core // 2, core % 2
    r0 = half * HALF
    x = inputs["L0_attn_x"]
    m = {
        "oTd": oT_core,
        "xres": _f32(x[b, r0:r0 + HALF]),
        "wo": _f32(inputs["L0_attn_Wo"]),
        "clsT": _tile128(_f32(inputs["L0_ffn_classifier_w"].T)),
        "ebias": _f32(np.tile(np.asarray(inputs["L0_ffn_e_score_correction_bias"]).reshape(1, 12), (128, 1))),
        "gwT": _f32(np.asarray(inputs["L0_ffn_gate_w"]).transpose(0, 2, 1).reshape(8, 16, 128, 1024)),
        "uwT": _f32(np.asarray(inputs["L0_ffn_up_w"]).transpose(0, 2, 1).reshape(8, 16, 128, 1024)),
        "dwT": _f32(np.asarray(inputs["L0_ffn_down_w"]).transpose(0, 2, 1).reshape(8, 8, 128, 2048)),
        "ident": IDENT,
    }
    if oT_core is None:
        m.pop("oTd"); m.pop("xres")
    return m


def prepC(inputs, core, y2_core):
    half = core % 2
    cqr, sqr, ckr, skr, a32 = hostC_consts(half)
    m = {
        "y2i": y2_core,
        "wq1": _f32(inputs["L1_attn_Wq"]), "wk1": _f32(inputs["L1_attn_Wk"]),
        "wv1": _f32(inputs["L1_attn_Wv"]),
        "wg1": _tile128(_f32(inputs["L1_attn_Wg"])),
        "cqr": cqr, "sqr": sqr, "ckr": ckr, "skr": skr, "a32d": a32, "ident": IDENT,
    }
    if y2_core is None:
        m.pop("y2i")
    return m


def build_D():
    nc = bass.Bass("TRN2", target_bir_lowering=False)
    qT8 = nc.dram_tensor("qT8", [8, 128, S_], BF16, kind="ExternalInput").ap()
    kT8 = nc.dram_tensor("kT8", [8, 128, S_], BF16, kind="ExternalInput").ap()
    v8 = nc.dram_tensor("v8", [8, 128, S_], BF16, kind="ExternalInput").ap()
    kc8 = nc.dram_tensor("kc8", [8, 128, 64], F32, kind="ExternalInput").ap()
    g8 = nc.dram_tensor("g8", [128, 32 * 24], F32, kind="ExternalInput").ap()
    cmpm_d = nc.dram_tensor("cmpm_d", [128, 32 * 64], BF16, kind="ExternalInput").ap()
    curb_d = nc.dram_tensor("curb_d", [128, 32 * 64], F32, kind="ExternalInput").ap()
    cd4_d = nc.dram_tensor("cd4_d", [128, 4 * 512], BF16, kind="ExternalInput").ap()
    swm_d = nc.dram_tensor("swm_d", [128, 2 * 256], BF16, kind="ExternalInput").ap()
    e8_d = nc.dram_tensor("e8_d", [64, 8 * 512], BF16, kind="ExternalInput").ap()
    a2_d = nc.dram_tensor("a2_d", [128, 32 * 64], BF16, kind="ExternalInput").ap()
    ident = nc.dram_tensor("ident", [128, 128], F32, kind="ExternalInput").ap()
    o8 = nc.dram_tensor("o8", [S_, 8 * 128], BF16, kind="ExternalOutput").ap()

    p = P()
    with ExitStack() as ctx:
        i_f = sbuf(ctx, nc, "i_f", [128, 128], F32)
        i_b = sbuf(ctx, nc, "i_b", [128, 128], BF16)
        cmpm = sbuf(ctx, nc, "cmpm", [128, 32 * 64], BF16)
        curb = sbuf(ctx, nc, "curb", [128, 32 * 64], F32)
        cd4 = sbuf(ctx, nc, "cd4", [128, 4 * 512], BF16)
        swm = sbuf(ctx, nc, "swm", [128, 2 * 256], BF16)
        e8s = sbuf(ctx, nc, "e8s", [64, 8 * 512], BF16)
        a2s = sbuf(ctx, nc, "a2s", [128, 32 * 64], BF16)
        g8s = sbuf(ctx, nc, "g8s", [128, 32 * 24], F32)
        QTu = sbuf(ctx, nc, "QTu", [128, S_], BF16)
        KTu = sbuf(ctx, nc, "KTu", [128, S_], BF16)
        Vu = sbuf(ctx, nc, "Vu", [128, S_], BF16)
        kcf = sbuf(ctx, nc, "kcf", [128, 64], F32)
        kcb = sbuf(ctx, nc, "kcb", [128, 64], BF16)
        vcb = sbuf(ctx, nc, "vcb", [64, 128], BF16)
        ecf = sbuf(ctx, nc, "ecf", [128, 64], F32)
        pcf = sbuf(ctx, nc, "pcf", [128, 64], F32)
        pcb = sbuf(ctx, nc, "pcb", [128, 64], BF16)
        impf = sbuf(ctx, nc, "impf", [128, 64], F32)
        nsb = sbuf(ctx, nc, "nsb", [128, 64], BF16)
        nstb = sbuf(ctx, nc, "nstb", [64, 128], BF16)
        pctb = sbuf(ctx, nc, "pctb", [64, 128], BF16)
        m8v = sbuf(ctx, nc, "m8v", [128, 8], F32)
        csum = sbuf(ctx, nc, "csum", [128, 1], F32)
        cs1 = sbuf(ctx, nc, "cs1", [128, 1], F32)
        rcv = sbuf(ctx, nc, "rcv", [128, 1], F32)
        eb = sbuf(ctx, nc, "eb", [128, 512], BF16)
        etb = sbuf(ctx, nc, "etb", [128, 512], BF16)
        ssl = sbuf(ctx, nc, "ssl", [128, 8], F32)
        ssv = sbuf(ctx, nc, "ssv", [128, 1], F32)
        rsl = sbuf(ctx, nc, "rsl", [128, 1], F32)
        gsl2 = sbuf(ctx, nc, "gsl2", [128, 1], F32)
        swb = sbuf(ctx, nc, "swb", [128, 256], BF16)
        swtb = sbuf(ctx, nc, "swtb", [128, 256], BF16)
        sws = sbuf(ctx, nc, "sws", [128, 1], F32)
        rsw = sbuf(ctx, nc, "rsw", [128, 1], F32)
        gsw2 = sbuf(ctx, nc, "gsw2", [128, 1], F32)
        oaf = sbuf(ctx, nc, "oaf", [128, 128], F32)
        oaf2 = sbuf(ctx, nc, "oaf2", [128, 128], F32)
        oaf3 = sbuf(ctx, nc, "oaf3", [128, 128], F32)
        ob = sbuf(ctx, nc, "ob", [128, 128], BF16)
        PD1 = psum(ctx, nc, "PD1", [128, 512], F32)
        PD2 = psum(ctx, nc, "PD2", [128, 512], BF16)
        PCM = psum(ctx, nc, "PCM", [128, 64], F32)
        OQ = psum(ctx, nc, "OQ", [128, 384], F32)
        PVC = psum(ctx, nc, "PVC", [64, 128], F32)
        PSW = psum(ctx, nc, "PSW", [128, 256], F32)

        p.d("sync",
            lambda h: h.dma_start(out=i_f[:], in_=ident),
            lambda h: h.dma_start(out=cmpm[:], in_=cmpm_d),
            lambda h: h.dma_start(out=curb[:], in_=curb_d),
            lambda h: h.dma_start(out=cd4[:], in_=cd4_d),
            lambda h: h.dma_start(out=swm[:], in_=swm_d),
            lambda h: h.dma_start(out=e8s[:], in_=e8_d),
            lambda h: h.dma_start(out=a2s[:], in_=a2_d),
            lambda h: h.dma_start(out=g8s[:], in_=g8))
        p.c("vector", lambda h: h.tensor_copy(i_b[:], i_f[:]))
        for u in range(8):
            p.d("sync",
                lambda h, u=u: h.dma_start(out=QTu[:], in_=qT8[u]),
                lambda h, u=u: h.dma_start(out=KTu[:], in_=kT8[u]),
                lambda h, u=u: h.dma_start(out=Vu[:], in_=v8[u]),
                lambda h, u=u: h.dma_start(out=kcf[:], in_=kc8[u]))
            p.c("vector", lambda h: h.tensor_copy(kcb[:], kcf[:]))
            p.c("tensor", *[(lambda h, kt=kt: h.matmul(
                PVC[:], a2s[:, kt * 64:(kt + 1) * 64], Vu[:, kt * 128:(kt + 1) * 128],
                start=(kt == 0), stop=(kt == 31))) for kt in range(32)])
            p.c("vector", lambda h: h.tensor_copy(vcb[:], PVC[:]))
            for qt in range(32):
                gofs = qt * 24 + u * 3
                p.c("tensor",
                    lambda h, qt=qt: h.matmul(PCM[:], QTu[:, qt * 128:(qt + 1) * 128], kcb[:],
                                              start=True, stop=False),
                    lambda h, qt=qt: h.matmul(PCM[:], i_b[:], cmpm[:, qt * 64:(qt + 1) * 64],
                                              start=False, stop=True))
                p.c("scalar", lambda h: h.activation(ecf[:], PCM[:], AF.Exp, accum_out=csum[:]))
                p.c("vector", lambda h: h.tensor_scalar(cs1[:], csum[:], 1e-30, None, ALU.add))
                p.c("vector", lambda h: h.reciprocal(rcv[:], cs1[:]))
                p.c("vector", lambda h: h.scalar_tensor_tensor(pcf[:], ecf[:], rcv[:], ecf[:], ALU.mult, ALU.bypass))
                p.c("vector", lambda h, qt=qt: h.tensor_tensor(impf[:], pcf[:], curb[:, qt * 64:(qt + 1) * 64], ALU.add))
                p.c("vector", lambda h: h.max(out=m8v[:], in_=impf[:]))
                p.c("vector", lambda h: h.scalar_tensor_tensor(nsb[:], impf[:], m8v[:, 7:8], impf[:], ALU.is_lt, ALU.bypass))
                p.c("vector", lambda h: h.tensor_copy(pcb[:], pcf[:]))
                p.c("tensor",
                    lambda h: h.matmul(PD2[0:64, 0:128], nsb[:], i_b[:],
                                       is_transpose=True, start=True, stop=False),
                    lambda h: h.matmul(PD2[0:64, 128:256], pcb[:], i_b[:],
                                       is_transpose=True, start=False, stop=True))
                p.c("vector",
                    lambda h: h.tensor_copy(nstb[:], PD2[0:64, 0:128]),
                    lambda h: h.tensor_copy(pctb[:], PD2[0:64, 128:256]))
                p.c("tensor", lambda h: h.matmul(OQ[:, 0:128], pctb[:], vcb[:], start=True, stop=True))
                p.c("vector", lambda h, gofs=gofs: h.scalar_tensor_tensor(
                    oaf[:], OQ[:, 0:128], g8s[:, gofs:gofs + 1], QTu[:, 0:128], ALU.mult, ALU.bypass))
                nk = qt // 4 + 1
                for kc in range(nk):
                    diag = (kc == qt // 4)
                    p.c("tensor",
                        lambda h, qt=qt, kc=kc: h.matmul(
                            PD1[:], QTu[:, qt * 128:(qt + 1) * 128], KTu[:, kc * 512:(kc + 1) * 512],
                            start=True, stop=False),
                        lambda h, kc=kc, diag=diag: h.matmul(
                            PD1[:], nstb[:], e8s[:, kc * 512:(kc + 1) * 512],
                            start=False, stop=not diag),
                        (lambda h, qt=qt: h.matmul(
                            PD1[:], i_b[:], cd4[:, (qt % 4) * 512:(qt % 4) * 512 + 512],
                            start=False, stop=True)) if diag else None)
                    p.c("scalar", lambda h, kc=kc: h.activation(
                        eb[:], PD1[:], AF.Exp, accum_out=ssl[:, kc:kc + 1]))
                    p.c("tensor", *[(lambda h, c=c: h.matmul(
                        PD2[:, c * 128:(c + 1) * 128], eb[:, c * 128:(c + 1) * 128], i_b[:],
                        is_transpose=True, start=(c == 0), stop=(c == 3))) for c in range(4)])
                    p.c("scalar", lambda h: h.copy(etb[:], PD2[:]))
                    p.c("tensor", *[(lambda h, kc=kc, c=c, nk=nk: h.matmul(
                        OQ[:, 128:256], etb[:, c * 128:(c + 1) * 128],
                        Vu[:, (kc * 4 + c) * 128:(kc * 4 + c) * 128 + 128],
                        start=(kc == 0 and c == 0), stop=(kc == nk - 1 and c == 3))) for c in range(4)])
                p.c("vector", lambda h, nk=nk: h.tensor_reduce(ssv[:], ssl[:, 0:nk], axis=AX.X, op=ALU.add))
                p.c("vector", lambda h: h.reciprocal(rsl[:], ssv[:]))
                p.c("vector", lambda h, gofs=gofs: h.tensor_tensor(gsl2[:], rsl[:], g8s[:, gofs + 1:gofs + 2], ALU.mult))
                p.c("vector", lambda h: h.scalar_tensor_tensor(
                    oaf2[:], OQ[:, 128:256], gsl2[:], oaf[:], ALU.mult, ALU.add))
                swt = [0, 0] if qt == 0 else [qt - 1, qt]
                var = 0 if qt == 0 else 1
                p.c("tensor",
                    lambda h, qt=qt, swt=swt: h.matmul(
                        PSW[:, 0:128], QTu[:, qt * 128:(qt + 1) * 128],
                        KTu[:, swt[0] * 128:swt[0] * 128 + 128],
                        start=True, stop=False),
                    lambda h, qt=qt, swt=swt: h.matmul(
                        PSW[:, 128:256], QTu[:, qt * 128:(qt + 1) * 128],
                        KTu[:, swt[1] * 128:swt[1] * 128 + 128],
                        start=False, stop=False),
                    lambda h, var=var: h.matmul(
                        PSW[:], i_b[:], swm[:, var * 256:(var + 1) * 256], start=False, stop=True))
                p.c("scalar", lambda h: h.activation(swb[:], PSW[:], AF.Exp, accum_out=sws[:]))
                p.c("tensor", *[(lambda h, c=c: h.matmul(
                    PD2[:, c * 128:(c + 1) * 128], swb[:, c * 128:(c + 1) * 128], i_b[:],
                    is_transpose=True, start=(c == 0), stop=(c == 1))) for c in range(2)])
                p.c("scalar", lambda h: h.copy(swtb[:], PD2[:, 0:256]))
                p.c("tensor", *[(lambda h, swt=swt, c=c: h.matmul(
                    OQ[:, 256:384], swtb[:, c * 128:(c + 1) * 128],
                    Vu[:, swt[c] * 128:swt[c] * 128 + 128],
                    start=(c == 0), stop=(c == 1))) for c in range(2)])
                p.c("vector", lambda h: h.reciprocal(rsw[:], sws[:]))
                p.c("vector", lambda h, gofs=gofs: h.tensor_tensor(gsw2[:], rsw[:], g8s[:, gofs + 2:gofs + 3], ALU.mult))
                p.c("vector", lambda h: h.scalar_tensor_tensor(
                    oaf3[:], OQ[:, 256:384], gsw2[:], oaf2[:], ALU.mult, ALU.add))
                p.c("vector", lambda h: h.tensor_copy(ob[:], oaf3[:]))
                p.d("gpsimd", lambda h, u=u, qt=qt: h.dma_start(
                    out=o8[qt * 128:(qt + 1) * 128, u * 128:(u + 1) * 128], in_=ob[:]))
        emit(nc, p)
    return nc


def prepD(core, qT_all, kT_all, v_all, g_all, kc_all):
    b, hhalf = core // 2, core % 2
    hbase = 8 * hhalf
    cmpm, curb, cd4, swm, e8, a2 = hostD_consts()
    qcat = np.concatenate([qT_all[2 * b], qT_all[2 * b + 1]], axis=2)
    kcat = np.concatenate([kT_all[2 * b], kT_all[2 * b + 1]], axis=2)
    vcat = np.concatenate([v_all[2 * b], v_all[2 * b + 1]], axis=0)
    gcat = np.concatenate([g_all[2 * b], g_all[2 * b + 1]], axis=0)
    kccat = np.concatenate([kc_all[2 * b], kc_all[2 * b + 1]], axis=1)
    qT8 = np.ascontiguousarray(qcat[hbase:hbase + 8])
    kT8 = np.ascontiguousarray(kcat[hbase:hbase + 8])
    v8 = np.stack([_tile128(np.ascontiguousarray(vcat[:, (hbase + u) * 128:(hbase + u + 1) * 128]))
                   for u in range(8)])
    kc8 = np.stack([np.concatenate([kc_all[2 * b][:, (hbase + u) * 32:(hbase + u + 1) * 32],
                                    kc_all[2 * b + 1][:, (hbase + u) * 32:(hbase + u + 1) * 32]],
                                   axis=1) for u in range(8)])
    g8m = np.zeros((128, 32, 8, 3), np.float32)
    for u in range(8):
        for gi in range(3):
            col = gcat[:, gi * 16 + hbase + u].reshape(32, 128)
            g8m[:, :, u, gi] = col.T
    g8m = _f32(g8m.reshape(128, 32 * 24))
    return {
        "qT8": _bf(qT8), "kT8": _bf(kT8), "v8": _bf(v8), "kc8": _f32(kc8), "g8": g8m,
        "cmpm_d": cmpm, "curb_d": curb, "cd4_d": cd4, "swm_d": swm, "e8_d": e8,
        "a2_d": a2, "ident": IDENT,
    }


def build_E():
    nc = bass.Bass("TRN2", target_bir_lowering=False)
    oi = nc.dram_tensor("oi", [HALF, D_], BF16, kind="ExternalInput").ap()
    y2r = nc.dram_tensor("y2r", [HALF, D_], F32, kind="ExternalInput").ap()
    wo1 = nc.dram_tensor("wo1", [D_, D_], F32, kind="ExternalInput").ap()
    wbin = nc.dram_tensor("wbin", [D_, D_], F32, kind="ExternalInput").ap()
    biasw = nc.dram_tensor("biasw", [1, D_], F32, kind="ExternalInput").ap()
    ident = nc.dram_tensor("ident", [128, 128], F32, kind="ExternalInput").ap()
    outd = nc.dram_tensor("outd", [HALF, D_], F32, kind="ExternalOutput").ap()
    y3h = nc.dram_tensor("y3h", [HALF, D_], F32, kind="Internal").ap()

    p = P()
    with ExitStack() as ctx:
        i_f = sbuf(ctx, nc, "i_f", [128, 128], F32)
        i_b = sbuf(ctx, nc, "i_b", [128, 128], BF16)
        y3T = sbuf(ctx, nc, "y3T", [128, 16 * 2048], BF16)
        hf = sbuf(ctx, nc, "hf", [128, 2048], F32)
        cen = sbuf(ctx, nc, "cen", [128, 2048], F32)
        ynf = sbuf(ctx, nc, "ynf", [128, 2048], F32)
        ynb = sbuf(ctx, nc, "ynb", [128, 2048], BF16)
        hsum = sbuf(ctx, nc, "hsum", [128, 1], F32)
        negm = sbuf(ctx, nc, "negm", [128, 1], F32)
        ss = sbuf(ctx, nc, "ss", [128, 1], F32)
        std = sbuf(ctx, nc, "std", [128, 1], F32)
        rr = sbuf(ctx, nc, "rr", [128, 1], F32)
        lnb = (hf, hsum, negm, cen, ss, std, rr)
        wf = sbuf(ctx, nc, "wf", [128, 2048], F32)
        PAf = psum(ctx, nc, "PAf", [128, 2048], F32)
        PBb = psum(ctx, nc, "PBb", [128, 2048], BF16)

        p.d("sync", lambda h: h.dma_start(out=i_f[:], in_=ident))
        p.c("vector", lambda h: h.tensor_copy(i_b[:], i_f[:]))
        with ExitStack() as c1:
            WOb = sbuf(c1, nc, "WOb", [128, 16 * 2048], BF16)
            otb = sbuf(c1, nc, "otb", [128, 2048], BF16)
            load_cast_w(p, wo1, WOb, wf, 16, 2048)
            for t in range(16):
                p.d("sync", lambda h, t=t: h.dma_start(out=otb[:], in_=oi[t * 128:(t + 1) * 128, :]))
                p.c("tensor", *[(lambda h, i=i: h.matmul(
                        PBb[:, i * 128:(i + 1) * 128], otb[:, i * 128:(i + 1) * 128], i_b[:],
                        is_transpose=True, start=(i % 8 == 0), stop=(i % 8 == 7))) for i in range(16)])
                p.c("vector", lambda h: h.tensor_copy(ynb[:], PBb[:]))
                p.c("tensor", *[(lambda h, n=n, i=i: h.matmul(
                    PAf[:, n * 512:(n + 1) * 512], ynb[:, i * 128:(i + 1) * 128],
                    WOb[:, i * 2048 + n * 512: i * 2048 + n * 512 + 512],
                    start=(i == 0), stop=(i == 15))) for n in range(4) for i in range(16)])
                p.d("sync", lambda h, t=t: h.dma_start(out=wf[:], in_=y2r[t * 128:(t + 1) * 128, :]))
                p.c("vector", lambda h: h.scalar_tensor_tensor(
                    hf[:], PAf[:], 1.0, wf[:], ALU.mult, ALU.add, accum_out=hsum[:]))
                ln_block(p, lnb, ynf)
                p.c("scalar", lambda h: h.activation(ynb[:], ynf[:], AF.Copy))
                p.c("tensor", *[(lambda h, i=i: h.matmul(
                        PBb[:, i * 128:(i + 1) * 128], ynb[:, i * 128:(i + 1) * 128], i_b[:],
                        is_transpose=True, start=(i % 8 == 0), stop=(i % 8 == 7))) for i in range(16)])
                p.c("vector", lambda h, t=t: h.tensor_copy(y3T[:, t * 2048:(t + 1) * 2048], PBb[:]))
                p.d("gpsimd", lambda h, t=t: h.dma_start(out=y3h[t * 128:(t + 1) * 128, :], in_=ynf[:]))
        with ExitStack() as c2:
            BWT = sbuf(c2, nc, "BWT", [128, 16 * 2048], BF16)
            sgnb = sbuf(c2, nc, "sgnb", [128, 2048], BF16)
            bwb = sbuf(c2, nc, "bwb", [128, 2048], BF16)
            asum = sbuf(c2, nc, "asum", [128, 1], F32)
            alpha = sbuf(c2, nc, "alpha", [128, 1], F32)
            bias_b = sbuf(c2, nc, "bias_b", [1, 2048], BF16)
            ones1 = sbuf(c2, nc, "ones1", [1, 128], BF16)
            p.d("sync", lambda h: h.dma_start(out=wf[:1, :], in_=biasw))
            p.c("vector", lambda h: h.tensor_copy(bias_b[:], wf[:1, :]))
            p.c("vector", lambda h: h.memset(ones1[:], 1.0))
            for ot in range(16):
                p.d("sync", lambda h, ot=ot: h.dma_start(out=wf[:], in_=wbin[ot * 128:(ot + 1) * 128, :]))
                p.c("vector", lambda h: h.tensor_reduce(
                    asum[:], wf[:], axis=AX.X, op=ALU.add, apply_absolute_value=True))
                p.c("vector", lambda h: h.tensor_scalar_mul(alpha[:], asum[:], 1.0 / 2048.0))
                p.c("scalar", lambda h: h.activation(sgnb[:], wf[:], AF.Sign))
                p.c("vector", lambda h: h.scalar_tensor_tensor(bwb[:], sgnb[:], alpha[:], sgnb[:], ALU.mult, ALU.bypass))
                p.c("tensor", *[(lambda h, i=i: h.matmul(
                        PBb[:, i * 128:(i + 1) * 128], bwb[:, i * 128:(i + 1) * 128], i_b[:],
                        is_transpose=True, start=(i % 8 == 0), stop=(i % 8 == 7))) for i in range(16)])
                p.c("vector", lambda h, ot=ot: h.tensor_copy(
                    BWT[:].rearrange("p (c o) -> p c o", c=16)[:, :, ot * 128:(ot + 1) * 128],
                    PBb[:].rearrange("p (c o) -> p c o", c=16)))
            for t in range(16):
                p.c("tensor",
                    *[(lambda h, t=t, n=n, i=i: h.matmul(
                        PAf[:, n * 512:(n + 1) * 512], y3T[:, t * 2048 + i * 128: t * 2048 + i * 128 + 128],
                        BWT[:, i * 2048 + n * 512: i * 2048 + n * 512 + 512],
                        start=(i == 0), stop=False)) for n in range(4) for i in range(16)],
                    *[(lambda h, n=n: h.matmul(
                        PAf[:, n * 512:(n + 1) * 512], ones1[:], bias_b[:, n * 512:(n + 1) * 512],
                        start=False, stop=True)) for n in range(4)])
                p.d("sync", lambda h, t=t: h.dma_start(out=wf[:], in_=y3h[t * 128:(t + 1) * 128, :]))
                p.c("vector", lambda h: h.scalar_tensor_tensor(
                    hf[:], PAf[:], 1.0, wf[:], ALU.mult, ALU.add, accum_out=hsum[:]))
                ln_block(p, lnb, ynf)
                p.d("gpsimd", lambda h, t=t: h.dma_start(out=outd[t * 128:(t + 1) * 128, :], in_=ynf[:]))
        emit(nc, p)
    return nc


def prepE(inputs, core, o_core, y2_core):
    return {
        "oi": o_core, "y2r": y2_core,
        "wo1": _f32(inputs["L1_attn_Wo"]),
        "wbin": _f32(inputs["L1_ffn_weight"]),
        "biasw": _f32(np.asarray(inputs["L1_ffn_bias"]).reshape(1, D_)),
        "ident": IDENT,
    }


_NC = {}


def _get(name, builder):
    if name not in _NC:
        _NC[name] = builder()
    return _NC[name]


def kernel(**inputs):
    cores = list(range(8))
    ncABC = _get("ABC", build_ABC)
    resC = run_bass_kernel_spmd(ncABC, [prepABC(inputs, c) for c in cores], core_ids=cores)
    y2 = [np.asarray(resC.results[c]["y2o"]) for c in cores]
    qT_all = [np.asarray(resC.results[c]["qTo"]) for c in cores]
    kT_all = [np.asarray(resC.results[c]["kTo"]) for c in cores]
    v_all = [np.asarray(resC.results[c]["vo"]) for c in cores]
    g_all = [np.asarray(resC.results[c]["go"]) for c in cores]
    kc_all = [np.asarray(resC.results[c]["kco"]) for c in cores]
    ncD = _get("D", build_D)
    resD = run_bass_kernel_spmd(ncD, [prepD(c, qT_all, kT_all, v_all, g_all, kc_all) for c in cores],
                                core_ids=cores)
    o8 = [np.asarray(resD.results[c]["o8"]) for c in cores]
    ncE = _get("E", build_E)
    in_mapsE = []
    for c in cores:
        b, half = c // 2, c % 2
        ofull = np.concatenate([o8[2 * b], o8[2 * b + 1]], axis=1)
        o_core = np.ascontiguousarray(ofull[half * HALF:(half + 1) * HALF]).astype(BFNP)
        in_mapsE.append(prepE(inputs, c, o_core, y2[c]))
    resE = run_bass_kernel_spmd(ncE, in_mapsE, core_ids=cores)
    out = np.zeros((B_, S_, D_), np.float32)
    for c in cores:
        b, half = c // 2, c % 2
        out[b, half * HALF:(half + 1) * HALF] = np.asarray(resE.results[c]["outd"])
    return out
